# version 26
# baseline (speedup 1.0000x reference)
import contextlib
import numpy as np

import concourse.bass as bass
import concourse.mybir as mybir
from concourse.bass_utils import run_bass_kernel_spmd

F32 = mybir.dt.float32
BF16 = mybir.dt.bfloat16
FP8 = mybir.dt.float8e4
AF = mybir.ActivationFunctionType
ALU = mybir.AluOpType
PM = mybir.MatmulPerfMode
AX = mybir.AxisListType

N_CORES = 8
SEQ, D, HID = 8192, 2048, 8192
HEADS, HD = 16, 128
JC = D // 128
HC = HID // 128
NB = 512
SLABS = D // NB
GELU_C = 0.7978845608028654
GELU_A = 0.044715
EPS = 1e-5


class Sem:
    def __init__(self, es, nc, name):
        self.h = es.enter_context(nc.semaphore(name))
        self.n = 0
        self.name = name


class Prog:
    ENGINES = ("sync", "tensor", "vector", "scalar", "gpsimd")

    def __init__(self):
        self.streams = {e: [] for e in self.ENGINES}

    def op(self, eng, emit, waits=(), inc=None, inc_n=1):
        w = [(s, v) for sv_ in waits if sv_ is not None
             for (s, v) in [sv_] if s is not None and v is not None and v > 0]
        self.streams[eng].append((emit, w, inc, inc_n))
        if inc is not None:
            inc.n += inc_n
            return (inc, inc.n)
        return None

    def emit(self, block):
        for eng in self.ENGINES:
            ops = self.streams[eng]
            if not ops:
                continue

            def runner(engobj, ops=ops):
                last = {}
                for emit, waits, inc, inc_n in ops:
                    for sem, val in waits:
                        if last.get(id(sem), 0) < val:
                            engobj.wait_ge(sem.h, val)
                            last[id(sem)] = val
                    ins = emit(engobj)
                    if inc is not None:
                        ins.then_inc(inc.h, inc_n)
            getattr(block, eng)(runner)


def build_program(tok):
    assert tok % 128 == 0
    T = tok // 128
    ND1 = min(512, tok)
    halves = [(i * ND1, ND1) for i in range(max(1, tok // ND1))]
    NH = len(halves)
    NG1 = 2 * NH

    nc = bass.Bass("TRN2", target_bir_lowering=False)

    x_d = nc.dram_tensor("x", [tok, D], F32, kind="ExternalInput").ap()
    wq_d = nc.dram_tensor("wqt", [SLABS, 128, JC, NB], F32, kind="ExternalInput").ap()
    wk_d = nc.dram_tensor("wkt", [SLABS, 128, JC, NB], F32, kind="ExternalInput").ap()
    wv_d = nc.dram_tensor("wvt", [SLABS, 128, JC, NB], F32, kind="ExternalInput").ap()
    wg0_d = nc.dram_tensor("wg0", [HID // 512, 128, JC, 512], F32, kind="ExternalInput").ap()
    wg1_d = nc.dram_tensor("wg1", [HID // 512, 128, JC, 512], F32, kind="ExternalInput").ap()
    wl_d = nc.dram_tensor("wl", [4, 128, HC, 512], F32, kind="ExternalInput").ap()
    id_d = nc.dram_tensor("ident", [128, 128], F32, kind="ExternalInput").ap()
    out_d = nc.dram_tensor("out", [tok, D], F32, kind="ExternalOutput").ap()

    q_d = nc.dram_tensor("q_scr", [tok, D], F32, kind="Internal").ap()
    k_d = nc.dram_tensor("k_scr", [tok, D], F32, kind="Internal").ap()
    v_d = nc.dram_tensor("v_scr", [tok, D], F32, kind="Internal").ap()
    p_d = nc.dram_tensor("p_scr", [tok, D], F32, kind="Internal").ap()
    qkv_d = [q_d, k_d, v_d]
    w_qkv = [wq_d, wk_d, wv_d]

    es = contextlib.ExitStack()
    P = Prog()

    xnT_q8 = es.enter_context(nc.sbuf_tensor("xnT_q8", [128, JC, tok], FP8))
    idt = es.enter_context(nc.sbuf_tensor("idt", [128, 128], F32))
    stats = es.enter_context(nc.sbuf_tensor("stats", [128, 4, 12], F32))
    PS = [es.enter_context(nc.psum_tensor("ps%d" % i, [128, 512], F32))
          for i in range(8)]

    sems = {}
    for name in ("sv", "sa", "s_idld", "s_xld0", "s_xld1", "s_xld2",
                 "s_xld3", "s_tr", "s_wld", "s_wpre",
                 "s_wcast", "s_bmm", "s_bev", "s_bst0", "s_bst1",
                 "s_adot", "s_exp", "s_cdone", "s_pst0", "s_pst1",
                 "s_wg00", "s_wg01", "s_wg10", "s_wg11",
                 "s_d1mm", "s_d1ev", "s_wll0", "s_wll1", "s_d2mm",
                 "s_d2ev", "s_fbf", "s_pld0", "s_pld1", "s_out0", "s_out1",
                 "s_evq8", "s_evlo"):
        sems[name] = Sem(es, nc, name)
    for ii in range(4):
        for par in range(2):
            sems["s_cld%d%d" % (ii, par)] = Sem(es, nc, "s_cld%d%d" % (ii, par))
    sems["sp"] = Sem(es, nc, "sp")
    g = sems
    SV, SA = g["sv"], g["sa"]
    SP = g["sp"]
    s_xld = [g["s_xld0"], g["s_xld1"], g["s_xld2"], g["s_xld3"]]
    s_bstp = [g["s_bst0"], g["s_bst1"]]
    s_pstp = [g["s_pst0"], g["s_pst1"]]
    s_wgp = [[g["s_wg00"], g["s_wg01"]], [g["s_wg10"], g["s_wg11"]]]
    s_wllp = [g["s_wll0"], g["s_wll1"]]
    s_pldp = [g["s_pld0"], g["s_pld1"]]
    s_outp = [g["s_out0"], g["s_out1"]]
    s_cldp = [[sems["s_cld%d%d" % (ii, par)] for par in range(2)]
              for ii in range(4)]

    tk_id = P.op("sync", lambda e: e.dma_start(out=idt[:], in_=id_d[:]),
                 inc=g["s_idld"], inc_n=16)

    esA = contextlib.ExitStack()
    xnT_hi_cm = nc.sbuf_tensor("xnT_hi", [128, JC, tok], BF16)
    xnT_lo_cm = nc.sbuf_tensor("xnT_lo", [128, JC, tok], BF16)
    xnT_hi = xnT_hi_cm.__enter__()
    xnT_lo = xnT_lo_cm.__enter__()
    wf32_cm = nc.sbuf_tensor("wf32", [128, JC, NB], F32)
    wf32 = wf32_cm.__enter__()
    xbuf = esA.enter_context(nc.sbuf_tensor("xbuf", [128, 4, D], F32))
    xnb = esA.enter_context(nc.sbuf_tensor("xnb", [128, 4, D], F32))
    bnb = esA.enter_context(nc.sbuf_tensor("bnb", [128, 4, 4, 6], F32))
    tk_wld_l = [None] * (3 * SLABS)
    tk_wld_l[0] = P.op("gpsimd",
                       lambda e: e.dma_start(out=wf32[:], in_=wq_d[0]),
                       inc=g["s_wpre"], inc_n=16)

    tk_xn = [None] * T
    tk_sq = [None] * T
    tk_s2 = [None] * T
    tk_trp = [None] * (4 * T)
    tk_hi = [None] * (4 * T)
    tk_q8 = [None] * (4 * T)
    tk_lo = [None] * (4 * T)

    for t in range(T):
        par = t % 4
        st_ = stats[:, par, :]
        tk_x = P.op("sync",
                    lambda e, par=par, t=t: e.dma_start(
                        out=xbuf[:, par, :], in_=x_d[t * 128:(t + 1) * 128, :]),
                    waits=[tk_xn[t - 4] if t >= 4 else None],
                    inc=s_xld[par], inc_n=16)
        tk_bn = None
        for cc in range(D // 512):
            tk_bn = P.op("vector",
                         lambda e, par=par, cc=cc: e.bn_stats(
                             bnb[:, par, cc, :],
                             xbuf[:, par, cc * 512:(cc + 1) * 512]),
                         waits=([tk_x, tk_xn[t - 4] if t >= 4 else None]
                                if cc == 0 else []),
                         inc=SV)
        tk_ag = P.op("vector",
                     lambda e, par=par, st_=st_: e.bn_aggr(
                         st_[:, 2:4],
                         bnb[:, par].rearrange("p a b -> p (a b)")),
                     waits=[tk_bn], inc=SV)
        tk_var = P.op("vector",
                      lambda e, st_=st_: e.tensor_scalar_add(
                          st_[:, 5:6], st_[:, 3:4], EPS),
                      waits=[tk_ag], inc=SV)
        tk_ln = P.op("scalar",
                     lambda e, st_=st_: e.activation(
                         out=st_[:, 6:7], in_=st_[:, 5:6], func=AF.Ln),
                     waits=[tk_var, tk_xn[t - 4] if t >= 4 else None],
                     inc=SA)
        tk_rstd = P.op("scalar",
                       lambda e, st_=st_: e.activation(
                           out=st_[:, 7:8], in_=st_[:, 6:7], func=AF.Exp,
                           scale=-0.5),
                       waits=[tk_ln], inc=SA)
        tk_nmr = P.op("vector",
                      lambda e, st_=st_: e.tensor_mul(
                          st_[:, 8:9], st_[:, 2:3], st_[:, 7:8]),
                      waits=[tk_rstd], inc=SV)
        tk_xn[t] = P.op("vector",
                        lambda e, par=par, st_=st_: e.tensor_scalar(
                            xnb[:, par, :], xbuf[:, par, :], st_[:, 7:8],
                            st_[:, 8:9], ALU.mult, ALU.subtract),
                        waits=[tk_nmr,
                               tk_trp[4 * (t - 4) + 3] if t >= 4 else None],
                        inc=SV)
        for p in range(4):
            kpk = 4 * t + p
            bank = 4 + (kpk % 4)
            tk = None
            for jj in range(4):
                j = 4 * p + jj
                tk = P.op("tensor",
                          lambda e, bank=bank, jj=jj, j=j, par=par: e.transpose(
                              PS[bank][:, jj * 128:(jj + 1) * 128],
                              xnb[:, par, j * 128:(j + 1) * 128], idt[:]),
                          waits=([tk_xn[t], tk_id,
                                  tk_q8[kpk - 4] if kpk >= 4 else None,
                                  tk_lo[kpk - 4] if kpk >= 4 else None]
                                 if jj == 0 else []),
                          inc=(g["s_tr"] if jj == 3 else None))
            tk_trp[kpk] = tk
            j0 = 4 * p
            tk_hi[kpk] = P.op("scalar",
                              lambda e, bank=bank, j0=j0, t=t: e.copy(
                                  out=xnT_hi[:, j0:j0 + 4, t * 128:(t + 1) * 128],
                                  in_=PS[bank][:].rearrange("p (c n) -> p c n", c=4)),
                              waits=[tk_trp[kpk]], inc=SA)
            tk_q8[kpk] = P.op("scalar",
                              lambda e, bank=bank, j0=j0, t=t: e.copy(
                                  out=xnT_q8[:, j0:j0 + 4, t * 128:(t + 1) * 128],
                                  in_=PS[bank][:].rearrange("p (c n) -> p c n", c=4)),
                              inc=g["s_evq8"])
            tk_lo[kpk] = P.op("vector",
                              lambda e, bank=bank, j0=j0, t=t: e.tensor_sub(
                                  xnT_lo[:, j0:j0 + 4, t * 128:(t + 1) * 128],
                                  PS[bank][:].rearrange("p (c n) -> p c n", c=4),
                                  xnT_hi[:, j0:j0 + 4, t * 128:(t + 1) * 128]),
                              waits=[tk_trp[kpk], tk_q8[kpk]],
                              inc=g["s_evlo"])
    esA.close()
    a_last_q8 = tk_q8[4 * T - 1]
    a_last_lo = tk_lo[4 * T - 1]
    a_tr_all = (g["s_tr"], g["s_tr"].n)

    esB = contextlib.ExitStack()
    whi = esB.enter_context(nc.sbuf_tensor("whi", [128, 2, JC, NB], BF16))
    wlo = esB.enter_context(nc.sbuf_tensor("wlo", [128, 2, JC, NB], BF16))
    stag = esB.enter_context(nc.sbuf_tensor("stag", [128, T, NB], F32))

    tk_wlo_l = [None] * (3 * SLABS)
    tk_bst = [None] * (3 * SLABS)
    tk_bev = {}

    def b_wload(si):
        wt_d = w_qkv[si // SLABS]
        s = si % SLABS
        return P.op("sync",
                    lambda e, wt_d=wt_d, s=s: e.dma_start(
                        out=wf32[:], in_=wt_d[s]),
                    waits=[tk_wlo_l[si - 1]],
                    inc=g["s_wld"], inc_n=16)

    for wi in range(3):
        wt_d = w_qkv[wi]
        for s in range(SLABS):
            si = wi * SLABS + s
            par = si % 2
            hw_c = [None, None]
            lw_c = [None, None]
            for hh in range(2):
                hw_c[hh] = P.op("vector",
                                lambda e, par=par, hh=hh: e.tensor_copy(
                                    whi[:, par, 8 * hh:8 * hh + 8, :],
                                    wf32[:, 8 * hh:8 * hh + 8, :]),
                                waits=([tk_wld_l[si],
                                        (g["s_bmm"], T * (si - 1)),
                                        a_last_q8 if si == 0 else None,
                                        a_last_lo if si == 0 else None,
                                        a_tr_all if si == 0 else None]
                                       if hh == 0 else []),
                                inc=g["s_wcast"])
            for hh in range(2):
                lw_c[hh] = P.op("vector",
                                lambda e, par=par, hh=hh: e.tensor_sub(
                                    wlo[:, par, 8 * hh:8 * hh + 8, :],
                                    wf32[:, 8 * hh:8 * hh + 8, :],
                                    whi[:, par, 8 * hh:8 * hh + 8, :]),
                                waits=[hw_c[hh]], inc=g["s_wcast"])
            tk_wlo_l[si] = lw_c[1]
            if si + 1 < 3 * SLABS:
                tk_wld_l[si + 1] = b_wload(si + 1)
            for t in range(T):
                grp = si * T + t
                bank = grp % 3
                if si == 0 and t == 0:
                    wmap = {0: [hw_c[0], a_last_q8, a_last_lo],
                            8: [hw_c[1]], 16: [lw_c[0]], 24: [lw_c[1]]}
                else:
                    wmap = {0: [tk_wlo_l[si], a_last_q8, a_last_lo,
                                tk_bev.get(grp - 2)]}
                tk_mm = None
                nmm = 0
                for (xa, wa) in ((xnT_hi, whi), (xnT_hi, wlo), (xnT_lo, whi)):
                    for j in range(JC):
                        tk_mm = P.op(
                            "tensor",
                            lambda e, bank=bank, xa=xa, wa=wa, j=j, t=t, par=par,
                                   st=(nmm == 0), sp=(nmm == 3 * JC - 1): e.matmul(
                                PS[bank][:, :NB],
                                xa[:, j, t * 128:(t + 1) * 128],
                                wa[:, par, j, :],
                                start=st, stop=sp),
                            waits=wmap.get(nmm, []),
                            inc=(g["s_bmm"] if nmm == 3 * JC - 1 else None))
                        nmm += 1
                tk_bev[grp] = P.op("scalar",
                                   lambda e, bank=bank, t=t: e.copy(
                                       out=stag[:, t, :], in_=PS[bank][:, :NB]),
                                   waits=[tk_mm, tk_bst[si - 1] if si >= 1 else None],
                                   inc=g["s_bev"])
            tk_bst[si] = P.op("gpsimd",
                 lambda e, wi=wi, s=s: e.dma_start(
                     out=qkv_d[wi][:, s * NB:(s + 1) * NB].rearrange(
                         "(t p) c -> p t c", p=128),
                     in_=stag[:]),
                 waits=[tk_bev[si * T + T - 1]],
                 inc=s_bstp[par], inc_n=16)
    esB.close()
    wf32_cm.__exit__(None, None, None)
    xnT_lo_cm.__exit__(None, None, None)
    xnT_hi_cm.__exit__(None, None, None)
    b_done_bst0 = (s_bstp[0], s_bstp[0].n)
    b_done_bst1 = (s_bstp[1], s_bstp[1].n)
    b_done_bev = (g["s_bev"], g["s_bev"].n)

    esD12 = contextlib.ExitStack()
    actT = esD12.enter_context(nc.sbuf_tensor("actT", [128, HC, tok], FP8))
    esC = contextlib.ExitStack()
    qb = esC.enter_context(nc.sbuf_tensor("qb", [128, 2, D], F32))
    kb = esC.enter_context(nc.sbuf_tensor("kb", [128, 2, D], F32))
    vb = esC.enter_context(nc.sbuf_tensor("vb", [128, D], F32))
    xb2 = esC.enter_context(nc.sbuf_tensor("xb2", [128, D], F32))
    Ab = esC.enter_context(nc.sbuf_tensor("Ab", [128, 256], F32))
    Eb = esC.enter_context(nc.sbuf_tensor("Eb", [128, 256], F32))
    nmx = esC.enter_context(nc.sbuf_tensor("nmx", [128, 16], F32))
    Zb = esC.enter_context(nc.sbuf_tensor("Zb", [128, 16], F32))
    iZ = esC.enter_context(nc.sbuf_tensor("iZ", [128, 16], F32))
    prodv = esC.enter_context(nc.sbuf_tensor("prodv", [128, 16, 128], F32))
    ab = esC.enter_context(nc.sbuf_tensor("ab", [128, D], F32))
    pstag = esC.enter_context(nc.sbuf_tensor("pstag", [128, D], F32))

    esD1 = contextlib.ExitStack()
    wgq = [esD1.enter_context(nc.sbuf_tensor("wgq%d" % m, [128, 2, JC, 512], FP8))
           for m in range(2)]
    gbf = esD1.enter_context(nc.sbuf_tensor("gbf", [128, 2, tok], BF16))
    ubf = esD1.enter_context(nc.sbuf_tensor("ubf", [128, 2, tok], BF16))
    cbuf = esD1.enter_context(nc.sbuf_tensor("cbuf", [128, 2, tok], BF16))
    wg_d = [wg0_d, wg1_d]

    tk_cdone = [None] * T
    tk_pst = [None] * T
    tk_cld = [[None] * 4 for _ in range(T)]

    def c_loads_qk(t):
        par = t % 2
        for ii, (dsrc, sb) in enumerate([(q_d, qb), (k_d, kb)]):
            tk_cld[t][ii] = P.op(
                "sync",
                lambda e, dsrc=dsrc, sb=sb, par=par, t=t: e.dma_start(
                    out=sb[:, par, :], in_=dsrc[t * 128:(t + 1) * 128, :]),
                waits=[b_done_bst0, b_done_bst1,
                       tk_cdone[t - 2] if t >= 2 else None],
                inc=s_cldp[ii][par], inc_n=16)

    def c_loads_vx(t):
        for ii, (dsrc, sb) in enumerate([(v_d, vb), (x_d, xb2)], start=2):
            tk_cld[t][ii] = P.op(
                "sync",
                lambda e, dsrc=dsrc, sb=sb, t=t: e.dma_start(
                    out=sb[:], in_=dsrc[t * 128:(t + 1) * 128, :]),
                waits=[b_done_bst0, b_done_bst1,
                       tk_cdone[t - 1] if t >= 1 else None],
                inc=s_cldp[ii][t % 2], inc_n=16)

    def c_tile(t):
        par = t % 2
        prev_step = tk_cdone[t - 1] if t >= 1 else None
        for gg in range(HEADS):
            first = True
            tk = None
            for h in range(gg, HEADS):
                col = h * 16 + gg
                w_ = [tk_cld[t][0], tk_cld[t][1],
                      (g["s_exp"], HEADS * t), prev_step] if first else []
                tk = P.op("vector",
                          lambda e, par=par, h=h, gg=gg, col=col: e.scalar_tensor_tensor(
                              out=prodv[:, h, :],
                              in0=qb[:, par, h * 128:(h + 1) * 128],
                              scalar=1.0,
                              in1=kb[:, par, gg * 128:(gg + 1) * 128],
                              op0=ALU.mult, op1=ALU.mult,
                              accum_out=Ab[:, col:col + 1]),
                          waits=w_,
                          inc=(SV if h == HEADS - 1 else None))
                first = False
            prev_step = tk if tk is not None else prev_step
            P.op("vector",
                 lambda e, h=gg: e.reduce_max(
                     nmx[:, h:h + 1], Ab[:, h * 16:h * 16 + h + 1],
                     axis=AX.X, negate=True),
                 waits=[prev_step], inc=g["s_adot"])
            P.op("scalar",
                 lambda e, h=gg: e.activation(
                     out=Eb[:, h * 16:h * 16 + h + 1],
                     in_=Ab[:, h * 16:h * 16 + h + 1],
                     func=AF.Exp, bias=nmx[:, h:h + 1]),
                 waits=[(g["s_adot"], HEADS * t + gg + 1),
                        tk_cdone[t - 1] if t >= 1 else None],
                 inc=g["s_exp"])
            if gg % 4 == 3:
                yield
        tk_z = None
        for h in range(HEADS):
            tk_z = P.op("vector",
                        lambda e, h=h: e.reduce_sum(
                            Zb[:, h:h + 1], Eb[:, h * 16:h * 16 + h + 1],
                            axis=AX.X),
                        waits=[(g["s_exp"], HEADS * t + h + 1)],
                        inc=(SV if h == HEADS - 1 else None))
        tk_iz = P.op("vector", lambda e: e.reciprocal(iZ[:], Zb[:]),
                     waits=[tk_z], inc=SV)
        prev_step = tk_iz
        for gg in range(HEADS):
            first = True
            tk = None
            for h in range(gg, HEADS):
                col = h * 16 + gg
                w_ = [tk_cld[t][2], prev_step] if first else []
                if gg == 0:
                    tk = P.op("vector",
                              lambda e, h=h, col=col: e.tensor_scalar_mul(
                                  ab[:, h * 128:(h + 1) * 128],
                                  vb[:, 0:128], Eb[:, col:col + 1]),
                              waits=w_,
                              inc=(SV if h == HEADS - 1 else None))
                else:
                    tk = P.op("vector",
                              lambda e, h=h, gg=gg, col=col: e.scalar_tensor_tensor(
                                  out=ab[:, h * 128:(h + 1) * 128],
                                  in0=vb[:, gg * 128:(gg + 1) * 128],
                                  scalar=Eb[:, col:col + 1],
                                  in1=ab[:, h * 128:(h + 1) * 128],
                                  op0=ALU.mult, op1=ALU.add),
                              waits=w_,
                              inc=(SV if h == HEADS - 1 else None))
                first = False
            prev_step = tk if tk is not None else prev_step
            if gg % 4 == 3:
                yield
        tk_sc = None
        for h in range(HEADS):
            tk_sc = P.op("vector",
                         lambda e, h=h: e.tensor_scalar_mul(
                             ab[:, h * 128:(h + 1) * 128],
                             ab[:, h * 128:(h + 1) * 128], iZ[:, h:h + 1]),
                         waits=[prev_step] if h == 0 else [],
                         inc=(SV if h == HEADS - 1 else None))
        tk_cdone[t] = P.op("vector",
                           lambda e, t=t: e.tensor_add(
                               pstag[:], xb2[:], ab[:]),
                           waits=[tk_sc, tk_cld[t][3],
                                  tk_pst[t - 1] if t >= 1 else None],
                           inc=g["s_cdone"])
        tk_pst[t] = P.op("sync",
                         lambda e, t=t: e.dma_start(
                             out=p_d[t * 128:(t + 1) * 128, :], in_=pstag[:]),
                         waits=[tk_cdone[t]],
                         inc=s_pstp[t % 2], inc_n=16)
        if t + 2 < T:
            c_loads_qk(t + 2)
        if t + 1 < T:
            c_loads_vx(t + 1)
        yield

    tk_abf = [None] * HC
    tk_cast = [None] * HC
    tk_d1mm = {}
    tk_wg_l = {}
    NSLAB1 = HID // 512

    def d1_wload(sl):
        spar = sl % 2
        for m in range(2):
            tk_wg_l[(sl, m)] = P.op(
                "gpsimd",
                lambda e, m=m, sl=sl, spar=spar: e.dma_start(
                    out=wgq[m][:, spar], in_=wg_d[m][sl]),
                waits=[b_done_bev,
                       b_done_bst0 if m == 1 else None,
                       b_done_bst1 if m == 1 else None,
                       (g["s_d1mm"], NG1 * 4 * (sl - 1))],
                inc=s_wgp[m][spar], inc_n=16)

    def d1_chunk(c):
        sl = c // 4
        cc = c % 4
        spar = sl % 2
        cpar = c % 2
        h0 = cc * 128
        if cc == 0 and sl + 1 < NSLAB1:
            d1_wload(sl + 1)
        for m in range(2):
            for hi_i, (hoff, hw) in enumerate(halves):
                bank = 4 * cpar + 2 * m + hi_i
                tk_mmg = None
                for jp in range(JC // 2):
                    tk_mmg = P.op(
                        "tensor",
                        lambda e, bank=bank, m=m, spar=spar, jp=jp, h0=h0,
                               hoff=hoff, hw=hw,
                               st=(jp == 0), sp=(jp == JC // 2 - 1): e.matmul(
                            PS[bank][:, :hw],
                            wgq[m][:, spar, 2 * jp:2 * jp + 2, h0:h0 + 128],
                            xnT_q8[:, 2 * jp:2 * jp + 2, hoff:hoff + hw],
                            start=st, stop=sp, perf_mode=PM.DoubleRow),
                        waits=([tk_wg_l[(sl, m)], (g["s_evq8"], g["s_evq8"].n),
                                tk_cast[c - 2] if c >= 2 else None,
                                b_done_bev]
                               if jp == 0 else []),
                        inc=(g["s_d1mm"] if jp == JC // 2 - 1 else None))
                tk_d1mm[(c, m, hi_i)] = tk_mmg
        tk_gbf = None
        for hi_i, (hoff, hw) in enumerate(halves):
            tk_gbf = P.op("scalar",
                          lambda e, cpar=cpar, hi_i=hi_i, hoff=hoff, hw=hw: e.copy(
                              out=gbf[:, cpar, hoff:hoff + hw],
                              in_=PS[4 * cpar + hi_i][:, :hw]),
                          waits=[tk_d1mm[(c, 0, hi_i)],
                                 tk_abf[c - 2] if c >= 2 else None,
                                 b_done_bst0 if c == 0 else None,
                                 b_done_bst1 if c == 0 else None],
                          inc=(SA if hi_i == NH - 1 else None))
        tk_x2g = P.op("scalar",
                      lambda e, cpar=cpar: e.activation(
                          out=cbuf[:, cpar, :], in_=gbf[:, cpar, :],
                          func=AF.Square),
                      waits=[tk_gbf, tk_abf[c - 2] if c >= 2 else None,
                             b_done_bst0 if c == 0 else None,
                             b_done_bst1 if c == 0 else None],
                      inc=SA)
        tk_v1 = P.op("vector",
                     lambda e, cpar=cpar: e.tensor_scalar(
                         cbuf[:, cpar, :], cbuf[:, cpar, :], GELU_A, 1.0,
                         ALU.mult, ALU.add),
                     waits=[tk_x2g], inc=SV)
        tk_ug = P.op("vector",
                     lambda e, cpar=cpar: e.tensor_mul(
                         cbuf[:, cpar, :], gbf[:, cpar, :], cbuf[:, cpar, :]),
                     waits=[tk_v1], inc=SV)
        tk_th = P.op("scalar",
                     lambda e, cpar=cpar: e.activation(
                         out=cbuf[:, cpar, :], in_=cbuf[:, cpar, :],
                         func=AF.Tanh, scale=GELU_C),
                     waits=[tk_ug], inc=SA)
        tk_p5 = P.op("vector",
                     lambda e, cpar=cpar: e.tensor_scalar(
                         cbuf[:, cpar, :], cbuf[:, cpar, :], 0.5, 0.5,
                         ALU.mult, ALU.add),
                     waits=[tk_th], inc=SV)
        tk_gv = P.op("vector",
                     lambda e, cpar=cpar: e.tensor_mul(
                         cbuf[:, cpar, :], gbf[:, cpar, :], cbuf[:, cpar, :]),
                     waits=[tk_p5], inc=SV)
        tk_ubf = None
        for hi_i, (hoff, hw) in enumerate(halves):
            tk_ubf = P.op("scalar",
                          lambda e, cpar=cpar, hi_i=hi_i, hoff=hoff, hw=hw: e.copy(
                              out=ubf[:, cpar, hoff:hoff + hw],
                              in_=PS[4 * cpar + 2 + hi_i][:, :hw]),
                          waits=[tk_d1mm[(c, 1, hi_i)],
                                 b_done_bst0 if c == 0 else None,
                                 b_done_bst1 if c == 0 else None],
                          inc=(SA if hi_i == NH - 1 else None))
        tk_abf[c] = P.op("vector",
                         lambda e, cpar=cpar: e.tensor_mul(
                             cbuf[:, cpar, :], cbuf[:, cpar, :],
                             ubf[:, cpar, :]),
                         waits=[tk_gv, tk_ubf,
                                tk_cast[c - 2] if c >= 2 else None],
                         inc=SV)
        tk_cast[c] = P.op("scalar",
                          lambda e, cpar=cpar, c=c: e.copy(
                              out=actT[:, c, :], in_=cbuf[:, cpar, :]),
                          waits=[tk_abf[c]],
                          inc=g["s_d1ev"])

    c_loads_qk(0)
    if T > 1:
        c_loads_qk(1)
    c_loads_vx(0)
    d1_wload(0)
    c_gens = [c_tile(t) for t in range(T)]
    c_state = {"i": 0}

    def advance_c():
        while c_state["i"] < T:
            try:
                next(c_gens[c_state["i"]])
                return
            except StopIteration:
                c_state["i"] += 1

    def flush_c():
        while c_state["i"] < T:
            advance_c()

    for cch in range(HC):
        advance_c()
        d1_chunk(cch)
    esD1.close()
    d1_done = (g["s_d1ev"], g["s_d1ev"].n)

    esD2 = contextlib.ExitStack()
    wlq2 = esD2.enter_context(nc.sbuf_tensor("wlq2", [128, 2, 32, 512], FP8))
    pbuf = esD2.enter_context(nc.sbuf_tensor("pbuf", [128, 2, 512], F32))
    fbf = esD2.enter_context(nc.sbuf_tensor("fbf", [128, 2, 512], BF16))
    ostg = esD2.enter_context(nc.sbuf_tensor("ostg", [128, 2, 512], F32))

    tk_wl_l = [None] * 8
    tk_d2ev = {}

    def d2_wload(k):
        ds, hh = k // 2, k % 2
        tk_wl_l[k] = P.op(
            "gpsimd",
            lambda e, ds=ds, hh=hh, k=k: e.dma_start(
                out=wlq2[:, k % 2],
                in_=wl_d[ds][:, hh * 32:(hh + 1) * 32, :]),
            waits=[d1_done, (g["s_d2mm"], T * (k - 1))],
            inc=s_wllp[k % 2], inc_n=16)

    d2_wload(0)
    d2_wload(1)
    for ds in range(4):
        for hh in range(2):
            k = 2 * ds + hh
            if k + 2 < 8:
                d2_wload(k + 2)
            for t in range(T):
                tk_mm = None
                for hp in range(16):
                    tk_mm = P.op(
                        "tensor",
                        lambda e, t=t, hh=hh, hp=hp, k=k,
                               st=(hh == 0 and hp == 0),
                               sp=(hh == 1 and hp == 15): e.matmul(
                            PS[t % 8][:],
                            actT[:, 32 * hh + 2 * hp:32 * hh + 2 * hp + 2,
                                 t * 128:(t + 1) * 128],
                            wlq2[:, k % 2, 2 * hp:2 * hp + 2, :],
                            start=st, stop=sp, perf_mode=PM.DoubleRow),
                        waits=([tk_wl_l[k], d1_done,
                                ((g["s_fbf"], T * (ds - 1) + t + 1)
                                 if (hh == 0 and ds >= 1) else None)]
                               if hp == 0 else []),
                        inc=(g["s_d2mm"] if hp == 15 else None))
            advance_c()
        for t in range(T):
            gd = ds * T + t
            while tk_pst[t] is None and c_state["i"] < T:
                advance_c()
            tk_pl = P.op("sync",
                         lambda e, t=t, ds=ds, gd=gd: e.dma_start(
                             out=pbuf[:, gd % 2, :],
                             in_=p_d[t * 128:(t + 1) * 128,
                                     ds * 512:(ds + 1) * 512]),
                         waits=[tk_pst[t], d1_done,
                                tk_d2ev.get(gd - 2)],
                         inc=s_pldp[gd % 2], inc_n=16)
            tk_fbf = P.op("vector",
                          lambda e, t=t, gd=gd: e.tensor_copy(
                              fbf[:, gd % 2, :], PS[t % 8][:]),
                          waits=[(g["s_d2mm"], T * (2 * ds + 1) + t + 1)],
                          inc=g["s_fbf"])
            tk_d2ev[gd] = P.op("vector",
                               lambda e, gd=gd: e.tensor_add(
                                   ostg[:, gd % 2, :], pbuf[:, gd % 2, :],
                                   fbf[:, gd % 2, :]),
                               waits=[tk_fbf, tk_pl,
                                      ((s_outp[gd % 2], 16 * ((gd - 2) // 2 + 1))
                                       if gd >= 2 else None)],
                               inc=g["s_d2ev"])
            P.op("sync",
                 lambda e, t=t, ds=ds, gd=gd: e.dma_start(
                     out=out_d[t * 128:(t + 1) * 128,
                               ds * 512:(ds + 1) * 512],
                     in_=ostg[:, gd % 2, :]),
                 waits=[tk_d2ev[gd]],
                 inc=s_outp[gd % 2], inc_n=16)
            if t % 4 == 3:
                advance_c()
    flush_c()
    esD2.close()
    esC.close()
    esD12.close()

    P.op("sync", lambda e: e.nop(),
         waits=[(s_outp[0], s_outp[0].n), (s_outp[1], s_outp[1].n)])

    with nc.Block() as block:
        P.emit(block)
    es.close()
    return nc


_NC_CACHE = {}


def _get_nc(tok):
    if tok not in _NC_CACHE:
        _NC_CACHE[tok] = build_program(tok)
    return _NC_CACHE[tok]


def kernel(L0_attn_input_seq_embs, L0_attn_Wq, L0_attn_Wk, L0_attn_Wv,
           L0_ffn_w_gating, L0_ffn_w_linear):
    x = np.ascontiguousarray(
        np.asarray(L0_attn_input_seq_embs, dtype=np.float32).reshape(SEQ, D))
    def _slab(a, nslab, nwide):
        K, N = a.shape
        return np.ascontiguousarray(
            a.reshape(K // 128, 128, nslab, nwide).transpose(2, 1, 0, 3))

    wqt = _slab(np.asarray(L0_attn_Wq, dtype=np.float32).T, SLABS, NB)
    wkt = _slab(np.asarray(L0_attn_Wk, dtype=np.float32).T, SLABS, NB)
    wvt = _slab(np.asarray(L0_attn_Wv, dtype=np.float32).T, SLABS, NB)
    wg = np.asarray(L0_ffn_w_gating, dtype=np.float32)
    wg0 = _slab(wg[0], HID // 512, 512)
    wg1 = _slab(wg[1], HID // 512, 512)
    wl = _slab(np.asarray(L0_ffn_w_linear, dtype=np.float32), 4, 512)
    ident = np.eye(128, dtype=np.float32)

    tok = SEQ // N_CORES
    nc = _get_nc(tok)
    in_maps = [{
        "x": x[c * tok:(c + 1) * tok],
        "wqt": wqt, "wkt": wkt, "wvt": wvt,
        "wg0": wg0, "wg1": wg1, "wl": wl, "ident": ident,
    } for c in range(N_CORES)]
    res = run_bass_kernel_spmd(nc, in_maps, core_ids=list(range(N_CORES)))
    out = np.concatenate([np.asarray(res.results[c]["out"])
                          for c in range(N_CORES)], axis=0)
    return out.reshape(1, SEQ, D).astype(np.float32)


# version 27
# speedup vs baseline: 1.0093x; 1.0093x over previous
import contextlib
import numpy as np

import concourse.bass as bass
import concourse.mybir as mybir
from concourse.bass_utils import run_bass_kernel_spmd

F32 = mybir.dt.float32
BF16 = mybir.dt.bfloat16
FP8 = mybir.dt.float8e4
AF = mybir.ActivationFunctionType
ALU = mybir.AluOpType
PM = mybir.MatmulPerfMode
AX = mybir.AxisListType

N_CORES = 8
SEQ, D, HID = 8192, 2048, 8192
HEADS, HD = 16, 128
JC = D // 128
HC = HID // 128
NB = 512
SLABS = D // NB
GELU_C = 0.7978845608028654
GELU_A = 0.044715
EPS = 1e-5


class Sem:
    def __init__(self, es, nc, name):
        self.h = es.enter_context(nc.semaphore(name))
        self.n = 0
        self.name = name


class Prog:
    ENGINES = ("sync", "tensor", "vector", "scalar", "gpsimd")

    def __init__(self):
        self.streams = {e: [] for e in self.ENGINES}

    def op(self, eng, emit, waits=(), inc=None, inc_n=1):
        w = [(s, v) for sv_ in waits if sv_ is not None
             for (s, v) in [sv_] if s is not None and v is not None and v > 0]
        self.streams[eng].append((emit, w, inc, inc_n))
        if inc is not None:
            inc.n += inc_n
            return (inc, inc.n)
        return None

    def emit(self, block):
        for eng in self.ENGINES:
            ops = self.streams[eng]
            if not ops:
                continue

            def runner(engobj, ops=ops):
                last = {}
                for emit, waits, inc, inc_n in ops:
                    for sem, val in waits:
                        if last.get(id(sem), 0) < val:
                            engobj.wait_ge(sem.h, val)
                            last[id(sem)] = val
                    ins = emit(engobj)
                    if inc is not None:
                        ins.then_inc(inc.h, inc_n)
            getattr(block, eng)(runner)


def build_program(tok):
    assert tok % 128 == 0
    T = tok // 128
    ND1 = min(512, tok)
    halves = [(i * ND1, ND1) for i in range(max(1, tok // ND1))]
    NH = len(halves)
    NG1 = 2 * NH

    nc = bass.Bass("TRN2", target_bir_lowering=False)

    x_d = nc.dram_tensor("x", [tok, D], F32, kind="ExternalInput").ap()
    wq_d = nc.dram_tensor("wqt", [SLABS, 128, JC, NB], F32, kind="ExternalInput").ap()
    wk_d = nc.dram_tensor("wkt", [SLABS, 128, JC, NB], F32, kind="ExternalInput").ap()
    wv_d = nc.dram_tensor("wvt", [SLABS, 128, JC, NB], F32, kind="ExternalInput").ap()
    wg0_d = nc.dram_tensor("wg0", [HID // 512, 128, JC, 512], F32, kind="ExternalInput").ap()
    wg1_d = nc.dram_tensor("wg1", [HID // 512, 128, JC, 512], F32, kind="ExternalInput").ap()
    wl_d = nc.dram_tensor("wl", [4, 128, HC, 512], F32, kind="ExternalInput").ap()
    id_d = nc.dram_tensor("ident", [128, 128], F32, kind="ExternalInput").ap()
    out_d = nc.dram_tensor("out", [tok, D], F32, kind="ExternalOutput").ap()

    q_d = nc.dram_tensor("q_scr", [tok, D], F32, kind="Internal").ap()
    k_d = nc.dram_tensor("k_scr", [tok, D], F32, kind="Internal").ap()
    v_d = nc.dram_tensor("v_scr", [tok, D], F32, kind="Internal").ap()
    p_d = nc.dram_tensor("p_scr", [tok, D], F32, kind="Internal").ap()
    qkv_d = [q_d, k_d, v_d]
    w_qkv = [wq_d, wk_d, wv_d]

    es = contextlib.ExitStack()
    P = Prog()

    xnT_q8 = es.enter_context(nc.sbuf_tensor("xnT_q8", [128, JC, tok], FP8))
    idt = es.enter_context(nc.sbuf_tensor("idt", [128, 128], F32))
    stats = es.enter_context(nc.sbuf_tensor("stats", [128, 4, 12], F32))
    PS = [es.enter_context(nc.psum_tensor("ps%d" % i, [128, 512], F32))
          for i in range(8)]

    sems = {}
    for name in ("sv", "sa", "s_idld", "s_xld0", "s_xld1", "s_xld2",
                 "s_xld3", "s_tr", "s_wld", "s_wpre",
                 "s_wcast", "s_bmm", "s_bev", "s_bst0", "s_bst1",
                 "s_adot", "s_exp", "s_cdone", "s_pst0", "s_pst1",
                 "s_wg00", "s_wg01", "s_wg10", "s_wg11",
                 "s_d1mm", "s_d1ev", "s_wll0", "s_wll1", "s_d2mm",
                 "s_d2ev", "s_fbf", "s_pld0", "s_pld1", "s_out0", "s_out1",
                 "s_evq8", "s_evlo"):
        sems[name] = Sem(es, nc, name)
    for ii in range(4):
        for par in range(2):
            sems["s_cld%d%d" % (ii, par)] = Sem(es, nc, "s_cld%d%d" % (ii, par))
    sems["sp"] = Sem(es, nc, "sp")
    g = sems
    SV, SA = g["sv"], g["sa"]
    SP = g["sp"]
    s_xld = [g["s_xld0"], g["s_xld1"], g["s_xld2"], g["s_xld3"]]
    s_bstp = [g["s_bst0"], g["s_bst1"]]
    s_pstp = [g["s_pst0"], g["s_pst1"]]
    s_wgp = [[g["s_wg00"], g["s_wg01"]], [g["s_wg10"], g["s_wg11"]]]
    s_wllp = [g["s_wll0"], g["s_wll1"]]
    s_pldp = [g["s_pld0"], g["s_pld1"]]
    s_outp = [g["s_out0"], g["s_out1"]]
    s_cldp = [[sems["s_cld%d%d" % (ii, par)] for par in range(2)]
              for ii in range(4)]

    tk_id = P.op("sync", lambda e: e.dma_start(out=idt[:], in_=id_d[:]),
                 inc=g["s_idld"], inc_n=16)

    esA = contextlib.ExitStack()
    xnT_hi_cm = nc.sbuf_tensor("xnT_hi", [128, JC, tok], BF16)
    xnT_lo_cm = nc.sbuf_tensor("xnT_lo", [128, JC, tok], BF16)
    xnT_hi = xnT_hi_cm.__enter__()
    xnT_lo = xnT_lo_cm.__enter__()
    wf32_cm = nc.sbuf_tensor("wf32", [128, JC, NB], F32)
    wf32 = wf32_cm.__enter__()
    xbuf = esA.enter_context(nc.sbuf_tensor("xbuf", [128, 4, D], F32))
    xnb = esA.enter_context(nc.sbuf_tensor("xnb", [128, 4, D], F32))
    bnb = esA.enter_context(nc.sbuf_tensor("bnb", [128, 4, 4, 6], F32))
    tk_wld_l = [None] * (3 * SLABS)
    tk_wld_l[0] = P.op("gpsimd",
                       lambda e: e.dma_start(out=wf32[:], in_=wq_d[0]),
                       inc=g["s_wpre"], inc_n=16)

    tk_xn = [None] * T
    tk_sq = [None] * T
    tk_s2 = [None] * T
    tk_trp = [None] * (4 * T)
    tk_hi = [None] * (4 * T)
    tk_q8 = [None] * (4 * T)
    tk_lo = [None] * (4 * T)

    for t in range(T):
        par = t % 4
        st_ = stats[:, par, :]
        tk_x = P.op("sync",
                    lambda e, par=par, t=t: e.dma_start(
                        out=xbuf[:, par, :], in_=x_d[t * 128:(t + 1) * 128, :]),
                    waits=[tk_xn[t - 4] if t >= 4 else None],
                    inc=s_xld[par], inc_n=16)
        tk_bn = None
        for cc in range(D // 512):
            tk_bn = P.op("vector",
                         lambda e, par=par, cc=cc: e.bn_stats(
                             bnb[:, par, cc, :],
                             xbuf[:, par, cc * 512:(cc + 1) * 512]),
                         waits=([tk_x, tk_xn[t - 4] if t >= 4 else None]
                                if cc == 0 else []),
                         inc=SV)
        tk_ag = P.op("vector",
                     lambda e, par=par, st_=st_: e.bn_aggr(
                         st_[:, 2:4],
                         bnb[:, par].rearrange("p a b -> p (a b)")),
                     waits=[tk_bn], inc=SV)
        tk_var = P.op("vector",
                      lambda e, st_=st_: e.tensor_scalar_add(
                          st_[:, 5:6], st_[:, 3:4], EPS),
                      waits=[tk_ag], inc=SV)
        tk_ln = P.op("scalar",
                     lambda e, st_=st_: e.activation(
                         out=st_[:, 6:7], in_=st_[:, 5:6], func=AF.Ln),
                     waits=[tk_var, tk_xn[t - 4] if t >= 4 else None],
                     inc=SA)
        tk_rstd = P.op("scalar",
                       lambda e, st_=st_: e.activation(
                           out=st_[:, 7:8], in_=st_[:, 6:7], func=AF.Exp,
                           scale=-0.5),
                       waits=[tk_ln], inc=SA)
        tk_nmr = P.op("vector",
                      lambda e, st_=st_: e.tensor_mul(
                          st_[:, 8:9], st_[:, 2:3], st_[:, 7:8]),
                      waits=[tk_rstd], inc=SV)
        tk_xn[t] = P.op("vector",
                        lambda e, par=par, st_=st_: e.tensor_scalar(
                            xnb[:, par, :], xbuf[:, par, :], st_[:, 7:8],
                            st_[:, 8:9], ALU.mult, ALU.subtract),
                        waits=[tk_nmr,
                               tk_trp[4 * (t - 4) + 3] if t >= 4 else None],
                        inc=SV)
        for p in range(4):
            kpk = 4 * t + p
            bank = 4 + (kpk % 4)
            tk = None
            for jj in range(4):
                j = 4 * p + jj
                tk = P.op("tensor",
                          lambda e, bank=bank, jj=jj, j=j, par=par: e.transpose(
                              PS[bank][:, jj * 128:(jj + 1) * 128],
                              xnb[:, par, j * 128:(j + 1) * 128], idt[:]),
                          waits=([tk_xn[t], tk_id,
                                  tk_q8[kpk - 4] if kpk >= 4 else None,
                                  tk_lo[kpk - 4] if kpk >= 4 else None]
                                 if jj == 0 else []),
                          inc=(g["s_tr"] if jj == 3 else None))
            tk_trp[kpk] = tk
            j0 = 4 * p
            tk_hi[kpk] = P.op("scalar",
                              lambda e, bank=bank, j0=j0, t=t: e.copy(
                                  out=xnT_hi[:, j0:j0 + 4, t * 128:(t + 1) * 128],
                                  in_=PS[bank][:].rearrange("p (c n) -> p c n", c=4)),
                              waits=[tk_trp[kpk]], inc=SA)
            tk_q8[kpk] = P.op("scalar",
                              lambda e, bank=bank, j0=j0, t=t: e.copy(
                                  out=xnT_q8[:, j0:j0 + 4, t * 128:(t + 1) * 128],
                                  in_=PS[bank][:].rearrange("p (c n) -> p c n", c=4)),
                              inc=g["s_evq8"])
            tk_lo[kpk] = P.op("vector",
                              lambda e, bank=bank, j0=j0, t=t: e.tensor_sub(
                                  xnT_lo[:, j0:j0 + 4, t * 128:(t + 1) * 128],
                                  PS[bank][:].rearrange("p (c n) -> p c n", c=4),
                                  xnT_hi[:, j0:j0 + 4, t * 128:(t + 1) * 128]),
                              waits=[tk_trp[kpk], tk_q8[kpk]],
                              inc=g["s_evlo"])
    esA.close()
    a_last_q8 = tk_q8[4 * T - 1]
    a_last_lo = tk_lo[4 * T - 1]
    a_tr_all = (g["s_tr"], g["s_tr"].n)

    esB = contextlib.ExitStack()
    whi = esB.enter_context(nc.sbuf_tensor("whi", [128, 2, JC, NB], BF16))
    wlo = esB.enter_context(nc.sbuf_tensor("wlo", [128, 2, JC, NB], BF16))
    stag = esB.enter_context(nc.sbuf_tensor("stag", [128, T, NB], F32))

    tk_wlo_l = [None] * (3 * SLABS)
    tk_bst = [None] * (3 * SLABS)
    tk_bev = {}

    def b_wload(si):
        wt_d = w_qkv[si // SLABS]
        s = si % SLABS
        return P.op("sync",
                    lambda e, wt_d=wt_d, s=s: e.dma_start(
                        out=wf32[:], in_=wt_d[s]),
                    waits=[tk_wlo_l[si - 1]],
                    inc=g["s_wld"], inc_n=16)

    for wi in range(3):
        wt_d = w_qkv[wi]
        for s in range(SLABS):
            si = wi * SLABS + s
            par = si % 2
            hw_c = [None, None]
            lw_c = [None, None]
            for hh in range(2):
                hw_c[hh] = P.op("vector",
                                lambda e, par=par, hh=hh: e.tensor_copy(
                                    whi[:, par, 8 * hh:8 * hh + 8, :],
                                    wf32[:, 8 * hh:8 * hh + 8, :]),
                                waits=([tk_wld_l[si],
                                        (g["s_bmm"], T * (si - 1)),
                                        a_last_q8 if si == 0 else None,
                                        a_last_lo if si == 0 else None,
                                        a_tr_all if si == 0 else None]
                                       if hh == 0 else []),
                                inc=g["s_wcast"])
            for hh in range(2):
                lw_c[hh] = P.op("vector",
                                lambda e, par=par, hh=hh: e.tensor_sub(
                                    wlo[:, par, 8 * hh:8 * hh + 8, :],
                                    wf32[:, 8 * hh:8 * hh + 8, :],
                                    whi[:, par, 8 * hh:8 * hh + 8, :]),
                                waits=[hw_c[hh]], inc=g["s_wcast"])
            tk_wlo_l[si] = lw_c[1]
            if si + 1 < 3 * SLABS:
                tk_wld_l[si + 1] = b_wload(si + 1)
            for t in range(T):
                grp = si * T + t
                bank = grp % 3
                if si == 0 and t == 0:
                    wmap = {0: [hw_c[0], a_last_q8, a_last_lo],
                            8: [hw_c[1]], 16: [lw_c[0]], 24: [lw_c[1]]}
                else:
                    wmap = {0: [tk_wlo_l[si], a_last_q8, a_last_lo,
                                tk_bev.get(grp - 2)]}
                tk_mm = None
                nmm = 0
                for (xa, wa) in ((xnT_hi, whi), (xnT_hi, wlo), (xnT_lo, whi)):
                    for j in range(JC):
                        tk_mm = P.op(
                            "tensor",
                            lambda e, bank=bank, xa=xa, wa=wa, j=j, t=t, par=par,
                                   st=(nmm == 0), sp=(nmm == 3 * JC - 1): e.matmul(
                                PS[bank][:, :NB],
                                xa[:, j, t * 128:(t + 1) * 128],
                                wa[:, par, j, :],
                                start=st, stop=sp),
                            waits=wmap.get(nmm, []),
                            inc=(g["s_bmm"] if nmm == 3 * JC - 1 else None))
                        nmm += 1
                tk_bev[grp] = P.op("scalar",
                                   lambda e, bank=bank, t=t: e.copy(
                                       out=stag[:, t, :], in_=PS[bank][:, :NB]),
                                   waits=[tk_mm, tk_bst[si - 1] if si >= 1 else None],
                                   inc=g["s_bev"])
            tk_bst[si] = P.op("gpsimd",
                 lambda e, wi=wi, s=s: e.dma_start(
                     out=qkv_d[wi][:, s * NB:(s + 1) * NB].rearrange(
                         "(t p) c -> p t c", p=128),
                     in_=stag[:]),
                 waits=[tk_bev[si * T + T - 1]],
                 inc=s_bstp[par], inc_n=16)
    esB.close()
    wf32_cm.__exit__(None, None, None)
    xnT_lo_cm.__exit__(None, None, None)
    xnT_hi_cm.__exit__(None, None, None)
    b_done_bst0 = (s_bstp[0], s_bstp[0].n)
    b_done_bst1 = (s_bstp[1], s_bstp[1].n)
    b_done_bev = (g["s_bev"], g["s_bev"].n)

    esD12 = contextlib.ExitStack()
    actT = esD12.enter_context(nc.sbuf_tensor("actT", [128, HC, tok], FP8))
    esC = contextlib.ExitStack()
    qb = esC.enter_context(nc.sbuf_tensor("qb", [128, 2, D], F32))
    kb = esC.enter_context(nc.sbuf_tensor("kb", [128, 2, D], F32))
    vb = esC.enter_context(nc.sbuf_tensor("vb", [128, D], F32))
    xb2 = esC.enter_context(nc.sbuf_tensor("xb2", [128, D], F32))
    Ab = esC.enter_context(nc.sbuf_tensor("Ab", [128, 256], F32))
    Eb = esC.enter_context(nc.sbuf_tensor("Eb", [128, 256], F32))
    nmx = esC.enter_context(nc.sbuf_tensor("nmx", [128, 16], F32))
    Zb = esC.enter_context(nc.sbuf_tensor("Zb", [128, 16], F32))
    iZ = esC.enter_context(nc.sbuf_tensor("iZ", [128, 16], F32))
    prodv = esC.enter_context(nc.sbuf_tensor("prodv", [128, 16, 128], F32))
    ab = esC.enter_context(nc.sbuf_tensor("ab", [128, D], F32))
    pstag = esC.enter_context(nc.sbuf_tensor("pstag", [128, D], F32))

    esD1 = contextlib.ExitStack()
    wgq = [esD1.enter_context(nc.sbuf_tensor("wgq%d" % m, [128, 2, JC, 512], FP8))
           for m in range(2)]
    gbf = esD1.enter_context(nc.sbuf_tensor("gbf", [128, 2, tok], BF16))
    ubf = esD1.enter_context(nc.sbuf_tensor("ubf", [128, 2, tok], BF16))
    cbuf = esD1.enter_context(nc.sbuf_tensor("cbuf", [128, 2, tok], BF16))
    wg_d = [wg0_d, wg1_d]

    tk_cdone = [None] * T
    tk_pst = [None] * T
    tk_cld = [[None] * 4 for _ in range(T)]

    def c_loads_qk(t):
        par = t % 2
        for ii, (dsrc, sb) in enumerate([(q_d, qb), (k_d, kb)]):
            tk_cld[t][ii] = P.op(
                "sync",
                lambda e, dsrc=dsrc, sb=sb, par=par, t=t: e.dma_start(
                    out=sb[:, par, :], in_=dsrc[t * 128:(t + 1) * 128, :]),
                waits=[b_done_bst0, b_done_bst1,
                       tk_cdone[t - 2] if t >= 2 else None],
                inc=s_cldp[ii][par], inc_n=16)

    def c_loads_vx(t):
        for ii, (dsrc, sb) in enumerate([(v_d, vb), (x_d, xb2)], start=2):
            tk_cld[t][ii] = P.op(
                "sync",
                lambda e, dsrc=dsrc, sb=sb, t=t: e.dma_start(
                    out=sb[:], in_=dsrc[t * 128:(t + 1) * 128, :]),
                waits=[b_done_bst0, b_done_bst1,
                       tk_cdone[t - 1] if t >= 1 else None],
                inc=s_cldp[ii][t % 2], inc_n=16)

    def c_tile(t):
        par = t % 2
        prev_step = tk_cdone[t - 1] if t >= 1 else None
        for gg in range(HEADS):
            first = True
            tk = None
            for h in range(gg, HEADS):
                col = h * 16 + gg
                w_ = [tk_cld[t][0], tk_cld[t][1],
                      (g["s_exp"], HEADS * t), prev_step] if first else []
                tk = P.op("vector",
                          lambda e, par=par, h=h, gg=gg, col=col: e.scalar_tensor_tensor(
                              out=prodv[:, h, :],
                              in0=qb[:, par, h * 128:(h + 1) * 128],
                              scalar=1.0,
                              in1=kb[:, par, gg * 128:(gg + 1) * 128],
                              op0=ALU.mult, op1=ALU.mult,
                              accum_out=Ab[:, col:col + 1]),
                          waits=w_,
                          inc=(SV if h == HEADS - 1 else None))
                first = False
            prev_step = tk if tk is not None else prev_step
            P.op("vector",
                 lambda e, h=gg: e.reduce_max(
                     nmx[:, h:h + 1], Ab[:, h * 16:h * 16 + h + 1],
                     axis=AX.X, negate=True),
                 waits=[prev_step], inc=g["s_adot"])
            P.op("scalar",
                 lambda e, h=gg: e.activation(
                     out=Eb[:, h * 16:h * 16 + h + 1],
                     in_=Ab[:, h * 16:h * 16 + h + 1],
                     func=AF.Exp, bias=nmx[:, h:h + 1]),
                 waits=[(g["s_adot"], HEADS * t + gg + 1),
                        tk_cdone[t - 1] if t >= 1 else None],
                 inc=g["s_exp"])
            if gg % 4 == 3:
                yield
        tk_z = None
        for h in range(HEADS):
            tk_z = P.op("vector",
                        lambda e, h=h: e.reduce_sum(
                            Zb[:, h:h + 1], Eb[:, h * 16:h * 16 + h + 1],
                            axis=AX.X),
                        waits=[(g["s_exp"], HEADS * t + h + 1)],
                        inc=(SV if h == HEADS - 1 else None))
        tk_iz = P.op("vector", lambda e: e.reciprocal(iZ[:], Zb[:]),
                     waits=[tk_z], inc=SV)
        prev_step = tk_iz
        for gg in range(HEADS):
            first = True
            tk = None
            for h in range(gg, HEADS):
                col = h * 16 + gg
                w_ = [tk_cld[t][2], prev_step] if first else []
                if gg == 0:
                    tk = P.op("vector",
                              lambda e, h=h, col=col: e.tensor_scalar_mul(
                                  ab[:, h * 128:(h + 1) * 128],
                                  vb[:, 0:128], Eb[:, col:col + 1]),
                              waits=w_,
                              inc=(SV if h == HEADS - 1 else None))
                else:
                    tk = P.op("vector",
                              lambda e, h=h, gg=gg, col=col: e.scalar_tensor_tensor(
                                  out=ab[:, h * 128:(h + 1) * 128],
                                  in0=vb[:, gg * 128:(gg + 1) * 128],
                                  scalar=Eb[:, col:col + 1],
                                  in1=ab[:, h * 128:(h + 1) * 128],
                                  op0=ALU.mult, op1=ALU.add),
                              waits=w_,
                              inc=(SV if h == HEADS - 1 else None))
                first = False
            prev_step = tk if tk is not None else prev_step
            if gg % 4 == 3:
                yield
        tk_sc = None
        for h in range(HEADS):
            tk_sc = P.op("vector",
                         lambda e, h=h: e.tensor_scalar_mul(
                             ab[:, h * 128:(h + 1) * 128],
                             ab[:, h * 128:(h + 1) * 128], iZ[:, h:h + 1]),
                         waits=[prev_step] if h == 0 else [],
                         inc=(SV if h == HEADS - 1 else None))
        tk_cdone[t] = P.op("vector",
                           lambda e, t=t: e.tensor_add(
                               pstag[:], xb2[:], ab[:]),
                           waits=[tk_sc, tk_cld[t][3],
                                  tk_pst[t - 1] if t >= 1 else None],
                           inc=g["s_cdone"])
        tk_pst[t] = P.op("sync",
                         lambda e, t=t: e.dma_start(
                             out=out_d[t * 128:(t + 1) * 128, :], in_=pstag[:]),
                         waits=[tk_cdone[t]],
                         inc=s_pstp[t % 2], inc_n=16)
        if t + 2 < T:
            c_loads_qk(t + 2)
        if t + 1 < T:
            c_loads_vx(t + 1)
        yield

    tk_abf = [None] * HC
    tk_cast = [None] * HC
    tk_d1mm = {}
    tk_wg_l = {}
    NSLAB1 = HID // 512

    def d1_wload(sl):
        spar = sl % 2
        for m in range(2):
            tk_wg_l[(sl, m)] = P.op(
                "gpsimd",
                lambda e, m=m, sl=sl, spar=spar: e.dma_start(
                    out=wgq[m][:, spar], in_=wg_d[m][sl]),
                waits=[b_done_bev,
                       b_done_bst0 if m == 1 else None,
                       b_done_bst1 if m == 1 else None,
                       (g["s_d1mm"], NG1 * 4 * (sl - 1))],
                inc=s_wgp[m][spar], inc_n=16)

    def d1_chunk(c):
        sl = c // 4
        cc = c % 4
        spar = sl % 2
        cpar = c % 2
        h0 = cc * 128
        if cc == 0 and sl + 1 < NSLAB1:
            d1_wload(sl + 1)
        for m in range(2):
            for hi_i, (hoff, hw) in enumerate(halves):
                bank = 4 * cpar + 2 * m + hi_i
                tk_mmg = None
                for jp in range(JC // 2):
                    tk_mmg = P.op(
                        "tensor",
                        lambda e, bank=bank, m=m, spar=spar, jp=jp, h0=h0,
                               hoff=hoff, hw=hw,
                               st=(jp == 0), sp=(jp == JC // 2 - 1): e.matmul(
                            PS[bank][:, :hw],
                            wgq[m][:, spar, 2 * jp:2 * jp + 2, h0:h0 + 128],
                            xnT_q8[:, 2 * jp:2 * jp + 2, hoff:hoff + hw],
                            start=st, stop=sp, perf_mode=PM.DoubleRow),
                        waits=([tk_wg_l[(sl, m)], (g["s_evq8"], g["s_evq8"].n),
                                tk_cast[c - 2] if c >= 2 else None,
                                b_done_bev]
                               if jp == 0 else []),
                        inc=(g["s_d1mm"] if jp == JC // 2 - 1 else None))
                tk_d1mm[(c, m, hi_i)] = tk_mmg
        tk_gbf = None
        for hi_i, (hoff, hw) in enumerate(halves):
            tk_gbf = P.op("scalar",
                          lambda e, cpar=cpar, hi_i=hi_i, hoff=hoff, hw=hw: e.copy(
                              out=gbf[:, cpar, hoff:hoff + hw],
                              in_=PS[4 * cpar + hi_i][:, :hw]),
                          waits=[tk_d1mm[(c, 0, hi_i)],
                                 tk_abf[c - 2] if c >= 2 else None,
                                 b_done_bst0 if c == 0 else None,
                                 b_done_bst1 if c == 0 else None],
                          inc=(SA if hi_i == NH - 1 else None))
        tk_x2g = P.op("scalar",
                      lambda e, cpar=cpar: e.activation(
                          out=cbuf[:, cpar, :], in_=gbf[:, cpar, :],
                          func=AF.Square),
                      waits=[tk_gbf, tk_abf[c - 2] if c >= 2 else None,
                             b_done_bst0 if c == 0 else None,
                             b_done_bst1 if c == 0 else None],
                      inc=SA)
        tk_v1 = P.op("vector",
                     lambda e, cpar=cpar: e.tensor_scalar(
                         cbuf[:, cpar, :], cbuf[:, cpar, :], GELU_A, 1.0,
                         ALU.mult, ALU.add),
                     waits=[tk_x2g], inc=SV)
        tk_ug = P.op("vector",
                     lambda e, cpar=cpar: e.tensor_mul(
                         cbuf[:, cpar, :], gbf[:, cpar, :], cbuf[:, cpar, :]),
                     waits=[tk_v1], inc=SV)
        tk_th = P.op("scalar",
                     lambda e, cpar=cpar: e.activation(
                         out=cbuf[:, cpar, :], in_=cbuf[:, cpar, :],
                         func=AF.Tanh, scale=GELU_C),
                     waits=[tk_ug], inc=SA)
        tk_p5 = P.op("vector",
                     lambda e, cpar=cpar: e.tensor_scalar(
                         cbuf[:, cpar, :], cbuf[:, cpar, :], 0.5, 0.5,
                         ALU.mult, ALU.add),
                     waits=[tk_th], inc=SV)
        tk_gv = P.op("vector",
                     lambda e, cpar=cpar: e.tensor_mul(
                         cbuf[:, cpar, :], gbf[:, cpar, :], cbuf[:, cpar, :]),
                     waits=[tk_p5], inc=SV)
        tk_ubf = None
        for hi_i, (hoff, hw) in enumerate(halves):
            tk_ubf = P.op("scalar",
                          lambda e, cpar=cpar, hi_i=hi_i, hoff=hoff, hw=hw: e.copy(
                              out=ubf[:, cpar, hoff:hoff + hw],
                              in_=PS[4 * cpar + 2 + hi_i][:, :hw]),
                          waits=[tk_d1mm[(c, 1, hi_i)],
                                 b_done_bst0 if c == 0 else None,
                                 b_done_bst1 if c == 0 else None],
                          inc=(SA if hi_i == NH - 1 else None))
        tk_abf[c] = P.op("vector",
                         lambda e, cpar=cpar: e.tensor_mul(
                             cbuf[:, cpar, :], cbuf[:, cpar, :],
                             ubf[:, cpar, :]),
                         waits=[tk_gv, tk_ubf,
                                tk_cast[c - 2] if c >= 2 else None],
                         inc=SV)
        tk_cast[c] = P.op("scalar",
                          lambda e, cpar=cpar, c=c: e.copy(
                              out=actT[:, c, :], in_=cbuf[:, cpar, :]),
                          waits=[tk_abf[c]],
                          inc=g["s_d1ev"])

    c_loads_qk(0)
    if T > 1:
        c_loads_qk(1)
    c_loads_vx(0)
    d1_wload(0)
    c_gens = [c_tile(t) for t in range(T)]
    c_state = {"i": 0}

    def advance_c():
        while c_state["i"] < T:
            try:
                next(c_gens[c_state["i"]])
                return
            except StopIteration:
                c_state["i"] += 1

    def flush_c():
        while c_state["i"] < T:
            advance_c()

    for cch in range(HC):
        advance_c()
        d1_chunk(cch)
    esD1.close()
    d1_done = (g["s_d1ev"], g["s_d1ev"].n)

    esD2 = contextlib.ExitStack()
    wlq2 = esD2.enter_context(nc.sbuf_tensor("wlq2", [128, 2, 32, 512], FP8))
    pbuf = esD2.enter_context(nc.sbuf_tensor("pbuf", [128, 2, 512], F32))
    fbf = esD2.enter_context(nc.sbuf_tensor("fbf", [128, 2, 512], BF16))
    ostg = esD2.enter_context(nc.sbuf_tensor("ostg", [128, 2, 512], F32))

    tk_wl_l = [None] * 8
    tk_d2ev = {}

    def d2_wload(k):
        ds, hh = k // 2, k % 2
        tk_wl_l[k] = P.op(
            "gpsimd",
            lambda e, ds=ds, hh=hh, k=k: e.dma_start(
                out=wlq2[:, k % 2],
                in_=wl_d[ds][:, hh * 32:(hh + 1) * 32, :]),
            waits=[d1_done, (g["s_d2mm"], T * (k - 1))],
            inc=s_wllp[k % 2], inc_n=16)

    d2_wload(0)
    d2_wload(1)
    for ds in range(4):
        for hh in range(2):
            k = 2 * ds + hh
            if k + 2 < 8:
                d2_wload(k + 2)
            for t in range(T):
                tk_mm = None
                for hp in range(16):
                    tk_mm = P.op(
                        "tensor",
                        lambda e, t=t, hh=hh, hp=hp, k=k,
                               st=(hh == 0 and hp == 0),
                               sp=(hh == 1 and hp == 15): e.matmul(
                            PS[t % 8][:],
                            actT[:, 32 * hh + 2 * hp:32 * hh + 2 * hp + 2,
                                 t * 128:(t + 1) * 128],
                            wlq2[:, k % 2, 2 * hp:2 * hp + 2, :],
                            start=st, stop=sp, perf_mode=PM.DoubleRow),
                        waits=([tk_wl_l[k], d1_done,
                                ((g["s_fbf"], T * (ds - 1) + t + 1)
                                 if (hh == 0 and ds >= 1) else None)]
                               if hp == 0 else []),
                        inc=(g["s_d2mm"] if hp == 15 else None))
            advance_c()
        for t in range(T):
            gd = ds * T + t
            while tk_pst[t] is None and c_state["i"] < T:
                advance_c()
            tk_fbf = P.op("vector",
                          lambda e, t=t, gd=gd: e.tensor_copy(
                              fbf[:, gd % 2, :], PS[t % 8][:]),
                          waits=[(g["s_d2mm"], T * (2 * ds + 1) + t + 1),
                                 ((s_outp[gd % 2], 16 * ((gd - 2) // 2 + 1))
                                  if gd >= 2 else None)],
                          inc=g["s_fbf"])
            tk_d2ev[gd] = P.op("gpsimd",
                               lambda e, t=t, ds=ds, gd=gd: e.dma_start(
                                   out=out_d[t * 128:(t + 1) * 128,
                                             ds * 512:(ds + 1) * 512],
                                   in_=fbf[:, gd % 2, :],
                                   accum_op=ALU.add),
                               waits=[tk_fbf, tk_pst[t]],
                               inc=s_outp[gd % 2], inc_n=16)
            if t % 4 == 3:
                advance_c()
    flush_c()
    esD2.close()
    esC.close()
    esD12.close()

    P.op("sync", lambda e: e.nop(),
         waits=[(s_outp[0], s_outp[0].n), (s_outp[1], s_outp[1].n)])

    with nc.Block() as block:
        P.emit(block)
    es.close()
    return nc


_NC_CACHE = {}


def _get_nc(tok):
    if tok not in _NC_CACHE:
        _NC_CACHE[tok] = build_program(tok)
    return _NC_CACHE[tok]


def kernel(L0_attn_input_seq_embs, L0_attn_Wq, L0_attn_Wk, L0_attn_Wv,
           L0_ffn_w_gating, L0_ffn_w_linear):
    x = np.ascontiguousarray(
        np.asarray(L0_attn_input_seq_embs, dtype=np.float32).reshape(SEQ, D))
    def _slab(a, nslab, nwide):
        K, N = a.shape
        return np.ascontiguousarray(
            a.reshape(K // 128, 128, nslab, nwide).transpose(2, 1, 0, 3))

    wqt = _slab(np.asarray(L0_attn_Wq, dtype=np.float32).T, SLABS, NB)
    wkt = _slab(np.asarray(L0_attn_Wk, dtype=np.float32).T, SLABS, NB)
    wvt = _slab(np.asarray(L0_attn_Wv, dtype=np.float32).T, SLABS, NB)
    wg = np.asarray(L0_ffn_w_gating, dtype=np.float32)
    wg0 = _slab(wg[0], HID // 512, 512)
    wg1 = _slab(wg[1], HID // 512, 512)
    wl = _slab(np.asarray(L0_ffn_w_linear, dtype=np.float32), 4, 512)
    ident = np.eye(128, dtype=np.float32)

    tok = SEQ // N_CORES
    nc = _get_nc(tok)
    in_maps = [{
        "x": x[c * tok:(c + 1) * tok],
        "wqt": wqt, "wkt": wkt, "wvt": wvt,
        "wg0": wg0, "wg1": wg1, "wl": wl, "ident": ident,
    } for c in range(N_CORES)]
    res = run_bass_kernel_spmd(nc, in_maps, core_ids=list(range(N_CORES)))
    out = np.concatenate([np.asarray(res.results[c]["out"])
                          for c in range(N_CORES)], axis=0)
    return out.reshape(1, SEQ, D).astype(np.float32)
